# version 04
# baseline (speedup 1.0000x reference)
import numpy as np
import ml_dtypes
from contextlib import ExitStack

import concourse.bass as bass
import concourse.mybir as mybir
from concourse.bass_utils import run_bass_kernel_spmd

F32 = mybir.dt.float32
BF16 = mybir.dt.bfloat16
I32 = mybir.dt.int32
AF = mybir.ActivationFunctionType
ALU = mybir.AluOpType
AX = mybir.AxisListType
BF = ml_dtypes.bfloat16

NCORE = 8
D = 1024
NF = D // 128
SEQ = 8192
B = 2
TOK = B * SEQ
CHUNK = TOK // NCORE
TB = 512
NTB = CHUNK // TB

ENGS = ("sync", "tensor", "vector", "scalar", "gpsimd")
_uid = [0]


def uid():
    _uid[0] += 1
    return _uid[0]


class Ctr:
    def __init__(self, sem):
        self.sem = sem
        self.v = 0

    def at(self):
        return (self.sem, self.v)


class Prog:

    def __init__(self, name="prog"):
        self.nc = bass.Bass("TRN2", target_bir_lowering=False)
        self.stack = ExitStack()
        self.ops = {e: [] for e in ENGS}
        self.ctrs = []

    def sbuf(self, name, shape, dt=F32, stack=None):
        return (stack or self.stack).enter_context(
            self.nc.sbuf_tensor(f"{name}_{uid()}", shape, dt))

    def psum(self, name, shape, dt=F32, stack=None):
        return (stack or self.stack).enter_context(
            self.nc.psum_tensor(f"{name}_{uid()}", shape, dt))

    def ctr(self, name="c"):
        sem = self.stack.enter_context(self.nc.semaphore(f"{name}_{uid()}"))
        c = Ctr(sem)
        self.ctrs.append(c)
        return c

    def on(self, eng, fn, waits=(), inc=None, inc_k=1):
        ws = []
        for w in waits:
            ws.append(w.at() if isinstance(w, Ctr) else w)
        self.ops[eng].append((fn, ws, inc.sem if inc else None, inc_k))
        if inc is not None:
            inc.v += inc_k

    def dma(self, eng, out_ap_fn, in_ap_fn, ctr, waits=()):
        self.on(eng, lambda e: e.dma_start(out_ap_fn(), in_ap_fn()),
                waits=waits, inc=ctr, inc_k=16)

    def emit(self):
        nc = self.nc
        ctrs = self.ctrs
        with nc.Block() as blk:
            @blk.gpsimd
            def _(g):
                for c in ctrs:
                    g.sem_clear(c.sem)
        ops = self.ops
        with nc.Block() as blk:
            for eng in ENGS:
                if not ops[eng]:
                    continue

                def mk(eng_name):
                    def body(e):
                        for fn, ws, sem, k in ops[eng_name]:
                            for (s, v) in ws:
                                e.wait_ge(s, v)
                            inst = fn(e)
                            if sem is not None:
                                assert inst is not None
                                inst.then_inc(sem, k)
                    return body
                getattr(blk, eng)(mk(eng))
        self.stack.close()
        return nc



def host_consts():
    cf = np.zeros((128, 8), np.float32)
    cf[:, 0] = 1.0
    cf[0:64, 1] = 1.0
    cf[64:128, 2] = 1.0
    cb = np.zeros((128, 132), np.float32)
    cb[:, 0:128] = np.eye(128, dtype=np.float32)
    cb[:, 128] = 1.0
    ct = np.zeros((2, 128), np.float32)
    ct[0, 0:64] = 1.0
    ct[1, 64:128] = 1.0
    onesr = np.ones((1, 128), np.float32)
    return {"c_f32": cf, "c_bf16": cb, "c_rows": ct, "c_ones": onesr}


def load_consts(p, ld):
    nc = p.nc
    cf_d = nc.dram_tensor("c_f32", [128, 8], F32, kind="ExternalInput")
    cb_d = nc.dram_tensor("c_bf16", [128, 132], F32, kind="ExternalInput")
    ct_d = nc.dram_tensor("c_rows", [2, 128], F32, kind="ExternalInput")
    co_d = nc.dram_tensor("c_ones", [1, 128], F32, kind="ExternalInput")
    cf = p.sbuf("cf", [128, 8], F32)
    cb = p.sbuf("cb", [128, 132], BF16)
    ct = p.sbuf("ct", [2, 128], F32)
    co = p.sbuf("co", [1, 128], F32)
    cob = p.sbuf("cob", [1, 128], BF16)
    p.dma("sync", lambda: cf[:, :], lambda: cf_d[:, :], ld)
    p.dma("gpsimd", lambda: cb[:, :], lambda: cb_d[:, :], ld)
    p.dma("sync", lambda: ct[:, :], lambda: ct_d[:, :], ld)
    p.dma("sync", lambda: co[:, :], lambda: co_d[:, :], ld)
    p.dma("gpsimd", lambda: cob[:, :], lambda: co_d[:, :], ld)
    return {
        "ones_col": cf[:, 0:1],
        "sel64": cf[:, 1:3],
        "ident": cb[:, 0:128],
        "ones_col_bf": cb[:, 128:129],
        "sel64T": ct[0:2, :],
        "ones_row": co[0:1, :],
        "ones_row_bf": cob[0:1, :],
    }



def emit_ln(p, C, x_sb, xn_sb, ld_x, done, T=CHUNK, xn_f32_sb=None,
            sum_row=None):
    ntb = T // TB
    ls = ExitStack()
    sq = p.sbuf("ln_sq", [128, TB * NF], F32)
    stat = p.psum("ln_stat", [1, 2 * TB], F32, stack=ls)
    ab = p.psum("ln_ab", [128, 2 * TB], F32, stack=ls)
    mu = p.sbuf("ln_mu", [1, TB], F32)
    rr = p.sbuf("ln_rr", [1, TB], F32)
    musq = p.sbuf("ln_musq", [1, TB], F32)
    brow = p.sbuf("ln_brow", [1, TB], F32)
    sq_done = p.ctr("lnsq")
    st_done = p.ctr("lnst")
    row_done = p.ctr("lnrow")
    ab_done = p.ctr("lnab")

    for tb in range(ntb):
        sl = slice(tb * TB, (tb + 1) * TB)
        for f in range(NF):
            fsl = slice(f * TB, (f + 1) * TB)
            p.on("scalar",
                 lambda e, f=f, sl=sl, fsl=fsl: e.activation(
                     sq[:, fsl], x_sb[f][:, sl], AF.Square),
                 waits=(ld_x, done), inc=sq_done)
        for f in range(NF):
            fsl = slice(f * TB, (f + 1) * TB)
            p.on("tensor",
                 lambda e, f=f, sl=sl: e.matmul(
                     stat[0:1, 0:TB], C["ones_col"], x_sb[f][:, sl],
                     start=(f == 0), stop=(f == NF - 1)),
                 waits=(ld_x, row_done),
                 inc=(st_done if f == NF - 1 else None))
            p.on("tensor",
                 lambda e, f=f, fsl=fsl: e.matmul(
                     stat[0:1, TB:2 * TB], C["ones_col"], sq[:, fsl],
                     start=(f == 0), stop=(f == NF - 1)),
                 waits=(sq_done, row_done),
                 inc=(st_done if f == NF - 1 else None))
        msl = slice(0, TB)
        p.on("vector",
             lambda e, msl=msl: e.tensor_scalar_mul(
                 mu[:, msl], stat[0:1, 0:TB], 1.0 / D),
             waits=(st_done, ab_done), inc=row_done)
        p.on("vector",
             lambda e, msl=msl: e.tensor_scalar_mul(
                 rr[:, msl], stat[0:1, TB:2 * TB], 1.0 / D),
             waits=(st_done, ab_done), inc=row_done)
        p.on("vector",
             lambda e, msl=msl: e.tensor_tensor(
                 musq[:, msl], mu[:, msl], mu[:, msl], ALU.mult),
             waits=(row_done,), inc=row_done)
        p.on("vector",
             lambda e, msl=msl: e.tensor_tensor(
                 rr[:, msl], rr[:, msl], musq[:, msl], ALU.subtract),
             waits=(row_done,), inc=row_done)
        p.on("scalar",
             lambda e, msl=msl: e.activation(
                 rr[:, msl], rr[:, msl], AF.Sqrt, bias=np.float32(1e-5)),
             waits=(row_done,), inc=row_done)
        p.on("vector",
             lambda e, msl=msl: e.reciprocal(rr[:, msl], rr[:, msl]),
             waits=(row_done,), inc=row_done)
        p.on("vector",
             lambda e, msl=msl: e.scalar_tensor_tensor(
                 brow[:, msl], mu[:, msl], -1.0, rr[:, msl],
                 ALU.mult, ALU.mult),
             waits=(row_done,), inc=row_done)
        p.on("tensor",
             lambda e, msl=msl: e.matmul(
                 ab[:, 0:TB], C["ones_row"], rr[:, msl],
                 start=True, stop=True),
             waits=(row_done, sq_done, done), inc=ab_done)
        p.on("tensor",
             lambda e, msl=msl: e.matmul(
                 ab[:, TB:2 * TB], C["ones_row"], brow[:, msl],
                 start=True, stop=True),
             waits=(row_done, sq_done, done), inc=ab_done)
        for f in range(NF):
            fsl = slice(f * TB, (f + 1) * TB)
            p.on("vector",
                 lambda e, f=f, sl=sl, fsl=fsl: e.tensor_tensor(
                     sq[:, fsl], x_sb[f][:, sl], ab[:, 0:TB], ALU.mult),
                 waits=(ab_done,), inc=sq_done)
            if xn_sb is not None:
                p.on("vector",
                     lambda e, f=f, sl=sl, fsl=fsl: e.tensor_tensor(
                         xn_sb[f][:, sl], sq[:, fsl], ab[:, TB:2 * TB],
                         ALU.add),
                     waits=(sq_done,), inc=done)
            if xn_f32_sb is not None:
                p.on("vector",
                     lambda e, f=f, sl=sl, fsl=fsl: e.tensor_tensor(
                         xn_f32_sb[f][:, sl], sq[:, fsl], ab[:, TB:2 * TB],
                         ALU.add),
                     waits=(sq_done,), inc=done)
    ls.close()


def emit_ln_dram(p, C, x_sb, out_d, sums_sb, ld_x, done, st, T=CHUNK):
    ntb = T // TB
    ls = ExitStack()
    sq = p.sbuf("ln_sq", [128, TB * NF], F32)
    red = p.sbuf("ln_red", [128, 1], F32)
    stat = p.psum("ln_stat", [1, 2 * TB], F32, stack=ls)
    ab = p.psum("ln_ab", [128, 2 * TB], F32, stack=ls)
    mu = p.sbuf("ln_mu", [1, TB], F32)
    rr = p.sbuf("ln_rr", [1, TB], F32)
    musq = p.sbuf("ln_musq", [1, TB], F32)
    brow = p.sbuf("ln_brow", [1, TB], F32)
    sq_done = p.ctr("lnsq")
    st_done = p.ctr("lnst")
    row_done = p.ctr("lnrow")
    ab_done = p.ctr("lnab")
    zero = p.ctr("lnz")
    p.on("vector", lambda e: e.memset(sums_sb[:, :], 0.0), inc=zero)

    for tb in range(ntb):
        sl = slice(tb * TB, (tb + 1) * TB)
        for f in range(NF):
            fsl = slice(f * TB, (f + 1) * TB)
            p.on("scalar",
                 lambda e, f=f, sl=sl, fsl=fsl: e.activation(
                     sq[:, fsl], x_sb[f][:, sl], AF.Square),
                 waits=(ld_x, done, st), inc=sq_done)
        for f in range(NF):
            fsl = slice(f * TB, (f + 1) * TB)
            p.on("tensor",
                 lambda e, f=f, sl=sl: e.matmul(
                     stat[0:1, 0:TB], C["ones_col"], x_sb[f][:, sl],
                     start=(f == 0), stop=(f == NF - 1)),
                 waits=(ld_x, row_done),
                 inc=(st_done if f == NF - 1 else None))
            p.on("tensor",
                 lambda e, f=f, fsl=fsl: e.matmul(
                     stat[0:1, TB:2 * TB], C["ones_col"], sq[:, fsl],
                     start=(f == 0), stop=(f == NF - 1)),
                 waits=(sq_done, row_done),
                 inc=(st_done if f == NF - 1 else None))
        msl = slice(0, TB)
        p.on("vector", lambda e: e.tensor_scalar_mul(
            mu[:, msl], stat[0:1, 0:TB], 1.0 / D),
            waits=(st_done, ab_done), inc=row_done)
        p.on("vector", lambda e: e.tensor_scalar_mul(
            rr[:, msl], stat[0:1, TB:2 * TB], 1.0 / D),
            waits=(st_done, ab_done), inc=row_done)
        p.on("vector", lambda e: e.tensor_tensor(
            musq[:, msl], mu[:, msl], mu[:, msl], ALU.mult),
            waits=(row_done,), inc=row_done)
        p.on("vector", lambda e: e.tensor_tensor(
            rr[:, msl], rr[:, msl], musq[:, msl], ALU.subtract),
            waits=(row_done,), inc=row_done)
        p.on("scalar", lambda e: e.activation(
            rr[:, msl], rr[:, msl], AF.Sqrt, bias=np.float32(1e-5)),
            waits=(row_done,), inc=row_done)
        p.on("vector", lambda e: e.reciprocal(rr[:, msl], rr[:, msl]),
             waits=(row_done,), inc=row_done)
        p.on("vector", lambda e: e.scalar_tensor_tensor(
            brow[:, msl], mu[:, msl], -1.0, rr[:, msl], ALU.mult, ALU.mult),
            waits=(row_done,), inc=row_done)
        p.on("tensor", lambda e: e.matmul(
            ab[:, 0:TB], C["ones_row"], rr[:, msl], start=True, stop=True),
            waits=(row_done, sq_done, done), inc=ab_done)
        p.on("tensor", lambda e: e.matmul(
            ab[:, TB:2 * TB], C["ones_row"], brow[:, msl],
            start=True, stop=True),
            waits=(row_done, sq_done, done), inc=ab_done)
        for f in range(NF):
            fsl = slice(f * TB, (f + 1) * TB)
            p.on("vector",
                 lambda e, f=f, sl=sl, fsl=fsl: e.tensor_tensor(
                     sq[:, fsl], x_sb[f][:, sl], ab[:, 0:TB], ALU.mult),
                 waits=(ab_done,), inc=sq_done)
            p.on("vector",
                 lambda e, f=f, fsl=fsl: e.tensor_tensor(
                     sq[:, fsl], sq[:, fsl], ab[:, TB:2 * TB], ALU.add),
                 waits=(sq_done, zero), inc=done)
            p.dma("sync",
                  lambda f=f, sl=sl: out_d[f * 128:(f + 1) * 128, sl],
                  lambda f=f, fsl=fsl: sq[:, fsl], st, waits=(done,))
            p.on("vector",
                 lambda e, f=f, fsl=fsl: e.reduce_sum(
                     red[:, :], sq[:, fsl], AX.X),
                 waits=(done,), inc=done)
            p.on("vector",
                 lambda e, f=f: e.tensor_tensor(
                     sums_sb[:, f:f + 1], sums_sb[:, f:f + 1], red[:, :],
                     ALU.add),
                 waits=(done,), inc=done)
    ls.close()


def emit_load_xT(p, name, T=CHUNK, ctr=None, dram=None):
    nc = p.nc
    d = dram if dram is not None else nc.dram_tensor(
        name, [D, T], F32, kind="ExternalInput")
    tiles = [p.sbuf(f"{name}_t{f}", [128, T], F32) for f in range(NF)]
    c = ctr or p.ctr("ldx")
    for f in range(NF):
        p.dma("sync", lambda f=f: tiles[f][:, :],
              lambda f=f: d[f * 128:(f + 1) * 128, :], c)
    return d, tiles, c



def build_A():
    p = Prog()
    nc = p.nc
    ld = p.ctr("ld")
    C = load_consts(p, ld)
    _, x_sb, _ = emit_load_xT(p, "xT", ctr=ld)
    w_d = nc.dram_tensor("qkv_w", [D, 3 * D], F32, kind="ExternalInput")
    b_d = nc.dram_tensor("qkv_b_cols", [128, 24], F32, kind="ExternalInput")
    qn_d = nc.dram_tensor("qnT", [D, CHUNK], F32, kind="ExternalOutput")
    kn_d = nc.dram_tensor("knT", [D, CHUNK], F32, kind="ExternalOutput")
    vn_d = nc.dram_tensor("vnT", [D, CHUNK], F32, kind="ExternalOutput")
    outs = {0: qn_d, 1: kn_d, 2: vn_d}

    bias_sb = p.sbuf("bias", [128, 24], F32)
    p.dma("scalar", lambda: bias_sb[:, :], lambda: b_d[:, :], ld)

    xn_sb = [p.sbuf(f"xn{f}", [128, CHUNK], BF16) for f in range(NF)]
    ln_done = p.ctr("ln")
    emit_ln(p, C, x_sb, xn_sb, ld, ln_done)

    wslab = [p.sbuf(f"wsl{k}", [128, 3 * D], BF16) for k in range(NF)]
    w_ld = p.ctr("wld")
    for k in range(NF):
        p.on("gpsimd",
             lambda e, k=k: e.dma_start(
                 wslab[k][:, :], w_d[k * 128:(k + 1) * 128, :]),
             inc=w_ld, inc_k=16)

    y_ps = [p.psum(f"y{t}", [128, TB], F32) for t in range(NTB)]
    nsq = p.psum("nsq", [32, TB], F32)
    bc = p.psum("bc", [128, TB], F32)
    qk_f32 = p.sbuf("qk_f32", [128, TB], F32)
    sqt = p.sbuf("sqt", [128, TB], F32)
    rnorm = p.sbuf("rnorm", [32, TB], F32)
    out_bf = p.sbuf("out_bf", [128, TB], F32)

    mm = p.ctr("mm")
    ev = p.ctr("ev")
    st = p.ctr("st")

    for m in range(24):
        s_part = m // NF
        fo = m % NF
        for k in range(NF):
            for tb in range(NTB):
                sl = slice(tb * TB, (tb + 1) * TB)
                waits = [ln_done, (w_ld.sem, 16 * (k + 1))]
                if k == 0:
                    waits.append(ev)
                p.on("tensor",
                     lambda e, m=m, k=k, tb=tb, sl=sl: e.matmul(
                         y_ps[tb][:, :],
                         wslab[k][:, m * 128:(m + 1) * 128],
                         xn_sb[k][:, sl],
                         start=(k == 0), stop=(k == NF - 1)),
                     waits=waits,
                     inc=(mm if k == NF - 1 else None))
        for tb in range(NTB):
            sl = slice(tb * TB, (tb + 1) * TB)
            bias_ap = bias_sb[:, m:m + 1]
            if s_part == 2:
                p.on("scalar",
                     lambda e, tb=tb, bias_ap=bias_ap: e.activation(
                         out_bf[:, :], y_ps[tb][:, :], AF.Identity,
                         bias=bias_ap),
                     waits=(mm, st), inc=ev)
                p.dma("sync",
                      lambda fo=fo, sl=sl: vn_d[fo * 128:(fo + 1) * 128, sl],
                      lambda: out_bf[:, :], st, waits=(ev,))
            else:
                p.on("scalar",
                     lambda e, tb=tb, bias_ap=bias_ap: e.activation(
                         qk_f32[:, :], y_ps[tb][:, :], AF.Identity,
                         bias=bias_ap),
                     waits=(mm, st, ev), inc=ev)
                p.on("scalar",
                     lambda e: e.activation(sqt[:, :], qk_f32[:, :],
                                            AF.Square),
                     waits=(ev,), inc=ev)
                p.on("tensor",
                     lambda e: e.matmul(nsq[0:2, :], C["sel64"], sqt[:, :],
                                        start=True, stop=True),
                     waits=(ev,), inc=mm)
                p.on("scalar",
                     lambda e: e.activation(rnorm[0:2, :], nsq[0:2, :],
                                            AF.Sqrt),
                     waits=(mm,), inc=ev)
                p.on("vector",
                     lambda e: e.tensor_scalar_max(rnorm[0:2, :],
                                                   rnorm[0:2, :], 1e-12),
                     waits=(ev,), inc=ev)
                p.on("vector",
                     lambda e: e.reciprocal(rnorm[0:2, :], rnorm[0:2, :]),
                     waits=(ev,), inc=ev)
                p.on("tensor",
                     lambda e: e.matmul(bc[:, :], C["sel64T"],
                                        rnorm[0:2, :], start=True, stop=True),
                     waits=(ev,), inc=mm)
                p.on("vector",
                     lambda e: e.tensor_tensor(out_bf[:, :], qk_f32[:, :],
                                               bc[:, :], ALU.mult),
                     waits=(mm,), inc=ev)
                p.dma("sync",
                      lambda fo=fo, sl=sl, s_part=s_part:
                          outs[s_part][fo * 128:(fo + 1) * 128, sl],
                      lambda: out_bf[:, :], st, waits=(ev,))
    p.on("gpsimd", lambda e: e.nop(), waits=(st,))
    return p.emit()



def emit_wslabs(p, dram, nk, M, ctr, name="wsl", row0=0):
    slabs = [p.sbuf(f"{name}{k}", [128, M], BF16) for k in range(nk)]
    for k in range(nk):
        p.on("gpsimd",
             lambda e, k=k: e.dma_start(
                 slabs[k][:, :],
                 dram[row0 + k * 128:row0 + (k + 1) * 128, :]),
             inc=ctr, inc_k=16)
    return slabs


def emit_mm_chain(p, lhsT_fn, rhs_fn, y_ps, nk, nm, ntb, mm, ev, waits_base,
                  evac_fn, wld=None):
    for m in range(nm):
        for k in range(nk):
            for tb in range(ntb):
                waits = list(waits_base)
                if wld is not None:
                    waits.append((wld.sem, 16 * (k + 1)))
                if k == 0:
                    waits.append(ev)
                lhsT = lhsT_fn(k, m)
                rhs = rhs_fn(k, tb)
                p.on("tensor",
                     lambda e, lhsT=lhsT, rhs=rhs, k=k, tb=tb: e.matmul(
                         y_ps[tb][:, :], lhsT, rhs,
                         start=(k == 0), stop=(k == nk - 1)),
                     waits=waits, inc=(mm if k == nk - 1 else None))
        for tb in range(ntb):
            evac_fn(m, tb)



def build_B():
    p = Prog()
    nc = p.nc
    ld = p.ctr("ld")
    C = load_consts(p, ld)
    _, x_sb, _ = emit_load_xT(p, "xT", ctr=ld)
    qn_d = nc.dram_tensor("qnT", [D, CHUNK], F32, kind="ExternalInput")
    knt_d = nc.dram_tensor("kn_tok", [SEQ, D], F32, kind="ExternalInput")
    vnt_d = nc.dram_tensor("vn_tok", [SEQ, D], F32, kind="ExternalInput")
    pw_d = nc.dram_tensor("proj_w", [D, D], F32, kind="ExternalInput")
    pb_d = nc.dram_tensor("proj_b_cols", [128, NF], F32, kind="ExternalInput")
    x1_d = nc.dram_tensor("x1T", [D, CHUNK], F32, kind="ExternalOutput")
    s0_d = nc.dram_tensor("s0T", [D, CHUNK], F32, kind="ExternalOutput")
    sums_d = nc.dram_tensor("sums", [128, NF], F32, kind="ExternalOutput")

    bias_sb = p.sbuf("bias", [128, NF], F32)
    p.dma("scalar", lambda: bias_sb[:, :], lambda: pb_d[:, :], ld)
    pslab = emit_wslabs(p, pw_d, NF, D, ld, "pj")
    kv_sb = p.sbuf("kv_sb", [128, 16 * 64], BF16)
    ao_sb = [p.sbuf(f"ao{f}", [128, CHUNK], BF16) for f in range(NF)]
    qn_stack = ExitStack()
    qn_sb = [p.sbuf(f"qn{f}", [128, CHUNK], BF16, stack=qn_stack)
             for f in range(NF)]
    for f in range(NF):
        p.dma("gpsimd", lambda f=f: qn_sb[f][:, :],
              lambda f=f: qn_d[f * 128:(f + 1) * 128, :], ld)

    tmp_stack = qn_stack
    ktile = [p.sbuf(f"kt{i}", [128, D], BF16, stack=tmp_stack) for i in range(2)]
    vtile = [p.sbuf(f"vt{i}", [128, D], BF16, stack=tmp_stack) for i in range(2)]
    ps_stack = ExitStack()
    kv_ps = p.psum("kv", [64, 16 * 64], F32, stack=ps_stack)
    kld = p.ctr("kld")
    kvmm = p.ctr("kvmm")
    mset = p.ctr("mset")
    zb64 = p.sbuf("zb64", [128, 64], BF16, stack=tmp_stack)
    zb512 = p.sbuf("zb512", [128, TB], BF16, stack=tmp_stack)
    zdone = p.ctr("zd")
    p.on("vector", lambda e: e.memset(zb64[:, :], 0.0), inc=zdone)
    p.on("vector", lambda e: e.memset(zb512[:, :], 0.0), inc=zdone)
    for half in range(2):
        p.on("tensor",
             lambda e, half=half: e.matmul(
                 kv_ps[0:64, half * 512:(half + 1) * 512], zb64[:, :],
                 zb512[:, :], start=True, stop=True, skip_group_check=True),
             waits=(zdone,), inc=mset)
    NTT = SEQ // 128
    for tt in range(NTT):
        waits = [] if tt < 2 else [(kvmm.sem, 16 * (tt - 1))]
        p.on("gpsimd",
             lambda e, tt=tt: e.dma_start(ktile[tt % 2][:, :],
                                          knt_d[tt * 128:(tt + 1) * 128, :]),
             waits=waits, inc=kld, inc_k=16)
        p.on("gpsimd",
             lambda e, tt=tt: e.dma_start(vtile[tt % 2][:, :],
                                          vnt_d[tt * 128:(tt + 1) * 128, :]),
             waits=waits, inc=kld, inc_k=16)
        for h in range(16):
            p.on("tensor",
                 lambda e, tt=tt, h=h: e.matmul(
                     kv_ps[0:64, h * 64:(h + 1) * 64],
                     ktile[tt % 2][:, h * 64:(h + 1) * 64],
                     vtile[tt % 2][:, h * 64:(h + 1) * 64],
                     start=False, stop=False, skip_group_check=True),
                 waits=[(kld.sem, 32 * (tt + 1)), mset],
                 inc=(kvmm if h == 15 else None), inc_k=16)
    kvev = p.ctr("kvev")
    p.on("vector", lambda e: e.tensor_copy(kv_sb[0:64, :], kv_ps[:, :]),
         waits=(kvmm,), inc=kvev)
    p.on("vector", lambda e: e.tensor_copy(kv_sb[64:128, :], kv_ps[:, :]),
         waits=(kvmm,), inc=kvev)

    op_ps = p.psum("op", [64, TB], F32, stack=ps_stack)
    aomm = p.ctr("aomm")
    aoev = p.ctr("aoev")
    for h in range(16):
        fo, po = h // 2, (h % 2) * 64
        for tb in range(NTB):
            sl = slice(tb * TB, (tb + 1) * TB)
            p.on("tensor",
                 lambda e, h=h, fo=fo, po=po, sl=sl: e.matmul(
                     op_ps[:, :], kv_sb[po:po + 64, h * 64:(h + 1) * 64],
                     qn_sb[fo][po:po + 64, sl], start=True, stop=True),
                 waits=(kvev, ld, aoev), inc=aomm)
            p.on("vector",
                 lambda e, fo=fo, po=po, sl=sl: e.tensor_copy(
                     ao_sb[fo][po:po + 64, sl], op_ps[:, :]),
                 waits=(aomm,), inc=aoev)

    y_ps = [p.psum(f"yp{t}", [128, TB], F32, stack=ps_stack)
            for t in range(NTB)]
    mm = p.ctr("mm")
    ev = p.ctr("ev")
    st = p.ctr("st")

    def evac(m, tb):
        sl = slice(tb * TB, (tb + 1) * TB)
        p.on("vector",
             lambda e, m=m, sl=sl, tb=tb: e.scalar_tensor_tensor(
                 x_sb[m][:, sl], y_ps[tb][:, :], bias_sb[:, m:m + 1],
                 x_sb[m][:, sl], ALU.add, ALU.add),
             waits=(mm,), inc=ev)
        p.dma("sync", lambda m=m, sl=sl: x1_d[m * 128:(m + 1) * 128, sl],
              lambda m=m, sl=sl: x_sb[m][:, sl], st, waits=(ev,))

    emit_mm_chain(
        p, lambda k, m: pslab[k][:, m * 128:(m + 1) * 128],
        lambda k, tb: ao_sb[k][:, tb * TB:(tb + 1) * TB],
        y_ps, NF, NF, NTB, mm, ev, [aoev, ld], evac)

    qn_stack.close()
    ps_stack.close()
    sums_sb = p.sbuf("sums_sb", [128, NF], F32)
    ln2 = p.ctr("ln2")
    emit_ln_dram(p, C, x_sb, s0_d, sums_sb, ev, ln2, st)
    p.dma("sync", lambda: sums_d[:, :], lambda: sums_sb[:, :], st,
          waits=(ln2,))
    p.on("gpsimd", lambda e: e.nop(), waits=(st,))
    return p.emit()



def build_D(ncyc=3, dbg=False, notwist=False, nochain=False, skip=(), nhalf=2):
    p = Prog()
    nc = p.nc
    ld = p.ctr("ld")
    C = load_consts(p, ld)
    s0_d = nc.dram_tensor("s0T", [D, CHUNK], F32, kind="ExternalInput")
    x1_d = nc.dram_tensor("x1T", [D, CHUNK], F32, kind="ExternalInput")
    sums_d = nc.dram_tensor("sums4", [4, 128, NF], F32, kind="ExternalInput")
    x2_d = nc.dram_tensor("x2T", [D, CHUNK], F32, kind="ExternalOutput")
    dbg_d = nc.dram_tensor("dbg", [4, CHUNK], F32, kind="ExternalOutput") if dbg else None
    cw1_d = nc.dram_tensor("cn_w1", [D, D], F32, kind="ExternalInput")
    cw2_d = nc.dram_tensor("cn_w2", [D, 512], F32, kind="ExternalInput")
    cw3_d = nc.dram_tensor("cn_w3", [512, 256], F32, kind="ExternalInput")
    cw4_d = nc.dram_tensor("cn_w4", [256, 1], F32, kind="ExternalInput")
    gw1_d = nc.dram_tensor("gc_w1", [D, 512], F32, kind="ExternalInput")
    gw2_d = nc.dram_tensor("gc_w2", [512, 256], F32, kind="ExternalInput")
    gw3_d = nc.dram_tensor("gc_w3", [256, 1], F32, kind="ExternalInput")
    cb1_d = nc.dram_tensor("cn_b1_cols", [128, 8], F32, kind="ExternalInput")
    cb2_d = nc.dram_tensor("cn_b2_cols", [128, 4], F32, kind="ExternalInput")
    cb3_d = nc.dram_tensor("cn_b3_cols", [128, 2], F32, kind="ExternalInput")
    cb4_d = nc.dram_tensor("cn_b4", [1, 1], F32, kind="ExternalInput")
    gb1_d = nc.dram_tensor("gc_b1_cols", [128, 4], F32, kind="ExternalInput")
    gb2_d = nc.dram_tensor("gc_b2_cols", [128, 2], F32, kind="ExternalInput")
    gb3_d = nc.dram_tensor("gc_b3", [1, 1], F32, kind="ExternalInput")
    ar_d = nc.dram_tensor("ar", [1, 1], F32, kind="ExternalInput")

    wld = p.ctr("wld")
    cw1 = emit_wslabs(p, cw1_d, 8, D, wld, "cw1")
    cw2 = emit_wslabs(p, cw2_d, 8, 512, wld, "cw2")
    cw3 = emit_wslabs(p, cw3_d, 4, 256, wld, "cw3")
    cw4 = emit_wslabs(p, cw4_d, 2, 1, wld, "cw4")
    gw1 = emit_wslabs(p, gw1_d, 8, 512, wld, "gw1")
    gw2 = emit_wslabs(p, gw2_d, 4, 256, wld, "gw2")
    gw3 = emit_wslabs(p, gw3_d, 2, 1, wld, "gw3")
    biases = p.sbuf("biases", [128, 32], F32)
    rowsc = p.sbuf("rowsc", [1, 16], F32)
    p.dma("scalar", lambda: biases[:, 0:8], lambda: cb1_d[:, :], ld)
    p.dma("scalar", lambda: biases[:, 8:12], lambda: cb2_d[:, :], ld)
    p.dma("scalar", lambda: biases[:, 12:14], lambda: cb3_d[:, :], ld)
    p.dma("scalar", lambda: biases[:, 16:20], lambda: gb1_d[:, :], ld)
    p.dma("scalar", lambda: biases[:, 20:22], lambda: gb2_d[:, :], ld)
    p.dma("scalar", lambda: rowsc[:, 0:1], lambda: cb4_d[:, :], ld)
    p.dma("scalar", lambda: rowsc[:, 1:2], lambda: gb3_d[:, :], ld)
    p.dma("scalar", lambda: rowsc[:, 2:3], lambda: ar_d[:, :], ld)

    sums_sb = p.sbuf("sums4", [128, 4 * NF], F32)
    for i in range(4):
        p.dma("sync", lambda i=i: sums_sb[:, i * NF:(i + 1) * NF],
              lambda i=i: sums_d[i, :, :], ld)
    gc = p.sbuf("gc", [128, NF], F32)
    gcb = p.sbuf("gcb", [128, NF], BF16)
    gev = p.ctr("gev")
    gmm = p.ctr("gmm")
    p.on("vector", lambda e: e.tensor_tensor(
        gc[:, :], sums_sb[:, 0:NF], sums_sb[:, NF:2 * NF], ALU.add),
        waits=(ld,), inc=gev)
    p.on("vector", lambda e: e.tensor_tensor(
        gc[:, :], gc[:, :], sums_sb[:, 2 * NF:3 * NF], ALU.add),
        waits=(gev,), inc=gev)
    p.on("vector", lambda e: e.tensor_tensor(
        gc[:, :], gc[:, :], sums_sb[:, 3 * NF:4 * NF], ALU.add),
        waits=(gev,), inc=gev)
    p.on("vector", lambda e: e.tensor_scalar_mul(gc[:, :], gc[:, :],
                                                 1.0 / SEQ),
         waits=(gev,), inc=gev)
    p.on("vector", lambda e: e.tensor_copy(gcb[:, :], gc[:, :]),
         waits=(gev,), inc=gev)
    g1_ps = p.psum("g1", [128, 4], F32)
    g2_ps = p.psum("g2", [128, 2], F32)
    g3_ps = p.psum("g3", [1, 1], F32)
    g1b = p.sbuf("g1b", [128, 4], BF16)
    g2b = p.sbuf("g2b", [128, 2], BF16)
    for m in range(4):
        for k in range(8):
            p.on("tensor",
                 lambda e, m=m, k=k: e.matmul(
                     g1_ps[:, m:m + 1], gw1[k][:, m * 128:(m + 1) * 128],
                     gcb[:, k:k + 1], start=(k == 0), stop=(k == 7),
                     skip_group_check=True),
                 waits=(gev, (wld.sem, wld.v)), inc=(gmm if k == 7 else None))
    for m in range(4):
        p.on("scalar",
             lambda e, m=m: e.activation(g1b[:, m:m + 1], g1_ps[:, m:m + 1],
                                         AF.Gelu, bias=biases[:, 16 + m:17 + m]),
             waits=(gmm,), inc=gev)
    for m in range(2):
        for k in range(4):
            p.on("tensor",
                 lambda e, m=m, k=k: e.matmul(
                     g2_ps[:, m:m + 1], gw2[k][:, m * 128:(m + 1) * 128],
                     g1b[:, k:k + 1], start=(k == 0), stop=(k == 3),
                     skip_group_check=True),
                 waits=(gev,), inc=(gmm if k == 3 else None))
    for m in range(2):
        p.on("scalar",
             lambda e, m=m: e.activation(g2b[:, m:m + 1], g2_ps[:, m:m + 1],
                                         AF.Gelu, bias=biases[:, 20 + m:21 + m]),
             waits=(gmm,), inc=gev)
    for k in range(2):
        p.on("tensor",
             lambda e, k=k: e.matmul(g3_ps[:, :], gw3[k][:, 0:1],
                                     g2b[:, k:k + 1], start=(k == 0),
                                     stop=(k == 1)),
             waits=(gev,), inc=(gmm if k == 1 else None))
    p.on("scalar", lambda e: e.activation(rowsc[:, 3:4], g3_ps[:, :],
                                          AF.Sigmoid, bias=rowsc[:, 1:2]),
         waits=(gmm,), inc=gev)
    p.on("vector", lambda e: e.tensor_scalar_mul(rowsc[:, 4:5],
                                                 rowsc[:, 3:4], 0.7),
         waits=(gev,), inc=gev)
    p.on("vector", lambda e: e.tensor_scalar_mul(rowsc[:, 5:6],
                                                 rowsc[:, 2:3], 2.0),
         waits=(gev,), inc=gev)
    p.on("vector", lambda e: e.tensor_scalar(rowsc[:, 6:7], rowsc[:, 2:3],
                                             -1.0, 0.1, ALU.mult, ALU.add),
         waits=(gev,), inc=gev)
    gf_done = p.ctr("gfd")
    p.on("vector", lambda e: e.tensor_copy(rowsc[:, 7:8], rowsc[:, 6:7]),
         waits=(gev,), inc=gf_done)

    TH = CHUNK // 2
    HTB = TH // TB
    sa = [p.sbuf(f"sa{f}", [128, TH], F32) for f in range(NF)]
    sb = [p.sbuf(f"sb{f}", [128, TH], F32) for f in range(NF)]
    sbf = [p.sbuf(f"sbf{f}", [128, TH], BF16) for f in range(NF)]
    h1b = [p.sbuf(f"h1b{f}", [128, TH], BF16) for f in range(NF)]
    h2b = [p.sbuf(f"h2b{f}", [128, TH], BF16) for f in range(4)]
    h3b = [p.sbuf(f"h3b{f}", [128, TH], BF16) for f in range(2)]
    x1h = [p.sbuf(f"x1h{f}", [128, TH], F32) for f in range(NF)]
    tfr = p.sbuf("tfr", [1, TH], F32)
    cplr = p.sbuf("cplr", [1, TH], F32)
    y_ps = [p.psum(f"yp{t}", [128, TB], F32) for t in range(2)]
    t4_ps = p.psum("t4", [1, TB], F32)
    cp_ps = p.psum("cp", [128, 2 * TB], F32)
    hld = p.ctr("hld")
    mm = p.ctr("mm")
    ev = p.ctr("ev")
    st = p.ctr("st")
    TWIST = {0: (4, 1.0), 1: (5, 1.0), 2: (6, -1.0), 3: (7, -1.0),
             4: (0, -1.0), 5: (1, -1.0), 6: (2, 1.0), 7: (3, 1.0)}

    for half in range(nhalf):
        hsl = slice(half * TH, (half + 1) * TH)
        for f in range(NF):
            p.on("sync",
                 lambda e, f=f, hsl=hsl: e.dma_start(
                     sa[f][:, :], s0_d[f * 128:(f + 1) * 128, hsl]),
                 waits=(ev, st), inc=hld, inc_k=16)
            p.on("sync",
                 lambda e, f=f, hsl=hsl: e.dma_start(
                     x1h[f][:, :], x1_d[f * 128:(f + 1) * 128, hsl]),
                 waits=(ev, st), inc=hld, inc_k=16)
        cast0 = p.ctr("cst")
        for f in range(NF):
            p.on("vector",
                 lambda e, f=f: e.tensor_copy(sbf[f][:, :], sa[f][:, :]),
                 waits=(hld, ev), inc=cast0)
        if dbg_d is not None and half == 0:
            p.dma("sync", lambda hsl=hsl: dbg_d[0:1, hsl],
                  lambda: sa[0][0:1, :], st, waits=(cast0,))
        cur, nxt = sa, sb
        for cyc in range(ncyc):
            if nochain:
                tw0 = p.ctr("tw0")
                for f in range(NF):
                    p.on("vector", lambda e, f=f, cur=cur, nxt=nxt:
                         e.tensor_copy(nxt[f][:, :], cur[f][:, :]),
                         waits=(st, ev, hld), inc=ev)
                cur, nxt = nxt, cur
                continue
            if "h1" in skip:
                pass
            def evac_h1(m, tb):
                p.on("scalar",
                     lambda e, m=m, tb=tb: e.activation(
                         h1b[m][:, tb * TB:(tb + 1) * TB], y_ps[tb][:, :],
                         AF.Gelu, bias=biases[:, m:m + 1]),
                     waits=(mm,), inc=ev)
            if "h1" not in skip:
                emit_mm_chain(p, lambda k, m: cw1[k][:, m * 128:(m + 1) * 128],
                              lambda k, tb: sbf[k][:, tb * TB:(tb + 1) * TB],
                              y_ps, 8, 8, HTB, mm, ev, [cast0, (wld.sem, wld.v)],
                              evac_h1)
            if dbg_d is not None and half == 0 and cyc == 0:
                p.dma("sync", lambda hsl=hsl: dbg_d[1:2, hsl],
                      lambda: sa[0][0:1, :], st, waits=(ev,))
            def evac_h2(m, tb):
                p.on("scalar",
                     lambda e, m=m, tb=tb: e.activation(
                         h2b[m][:, tb * TB:(tb + 1) * TB], y_ps[tb][:, :],
                         AF.Gelu, bias=biases[:, 8 + m:9 + m]),
                     waits=(mm,), inc=ev)
            if "h2" not in skip:
                emit_mm_chain(p, lambda k, m: cw2[k][:, m * 128:(m + 1) * 128],
                              lambda k, tb: h1b[k][:, tb * TB:(tb + 1) * TB],
                              y_ps, 8, 4, HTB, mm, ev, [ev], evac_h2)
            def evac_h3(m, tb):
                p.on("scalar",
                     lambda e, m=m, tb=tb: e.activation(
                         h3b[m][:, tb * TB:(tb + 1) * TB], y_ps[tb][:, :],
                         AF.Gelu, bias=biases[:, 12 + m:13 + m]),
                     waits=(mm,), inc=ev)
            if "h3" not in skip:
                emit_mm_chain(p, lambda k, m: cw3[k][:, m * 128:(m + 1) * 128],
                              lambda k, tb: h2b[k][:, tb * TB:(tb + 1) * TB],
                              y_ps, 4, 2, HTB, mm, ev, [ev], evac_h3)
            if "t4" in skip:
                p.on("vector", lambda e: e.memset(tfr[:, :], 0.5),
                     waits=(ev,), inc=ev)
            else:
                for tb in range(HTB):
                    for k in range(2):
                        p.on("tensor",
                             lambda e, k=k, tb=tb: e.matmul(
                                 t4_ps[:, :], cw4[k][:, 0:1],
                                 h3b[k][:, tb * TB:(tb + 1) * TB],
                                 start=(k == 0), stop=(k == 1)),
                             waits=(ev,), inc=(mm if k == 1 else None))
                    p.on("scalar",
                         lambda e, tb=tb: e.activation(
                             tfr[:, tb * TB:(tb + 1) * TB], t4_ps[:, :],
                             AF.Sigmoid, bias=rowsc[:, 0:1]),
                         waits=(mm,), inc=ev)
            if dbg_d is not None and cyc == 0:
                p.dma("sync",
                      lambda hsl=hsl: dbg_d[0:1, hsl],
                      lambda: tfr[:, :], st, waits=(ev,))
            p.on("vector",
                 lambda e: e.tensor_scalar(cplr[:, :], tfr[:, :], 0.3,
                                           rowsc[:, 4:5], ALU.mult, ALU.add),
                 waits=(ev, gf_done), inc=ev)
            p.on("vector",
                 lambda e: e.tensor_scalar(cplr[:, :], cplr[:, :],
                                           rowsc[:, 5:6], rowsc[:, 6:7],
                                           ALU.mult, ALU.add),
                 waits=(ev,), inc=ev)
            if dbg_d is not None and cyc == 0:
                p.dma("sync",
                      lambda hsl=hsl: dbg_d[1:2, hsl],
                      lambda: cplr[:, :], st, waits=(ev,))
            for tb in range(HTB):
                p.on("tensor",
                     lambda e, tb=tb: e.matmul(
                         cp_ps[:, tb * TB:(tb + 1) * TB], C["ones_row"],
                         cplr[:, tb * TB:(tb + 1) * TB],
                         start=True, stop=True),
                     waits=(ev,), inc=mm)
            if dbg_d is not None and cyc == 0:
                p.dma("sync", lambda hsl=hsl: dbg_d[2:3, hsl],
                      lambda cur=cur: cur[0][0:1, :], st, waits=(mm,))
            tw = p.ctr("tw")
            for f in range(NF):
                src, sgn = TWIST[f]
                if notwist:
                    for tb in range(HTB):
                        tsl = slice(tb * TB, (tb + 1) * TB)
                        p.on("vector",
                             lambda e, f=f, tsl=tsl, cur=cur, nxt=nxt:
                                 e.tensor_copy(nxt[f][:, tsl],
                                               cur[f][:, tsl]),
                             waits=(mm, st), inc=ev)
                    continue
                for tb in range(HTB):
                    tsl = slice(tb * TB, (tb + 1) * TB)
                    p.on("vector",
                         lambda e, f=f, src=src, tsl=tsl, cur=cur, nxt=nxt:
                             e.tensor_tensor(
                                 nxt[f][:, tsl],
                                 cur[src][:, tsl], cp_ps[:, tsl], ALU.mult),
                         waits=(mm, st), inc=tw)
                    p.on("vector",
                         lambda e, f=f, sgn=sgn, tsl=tsl, nxt=nxt:
                             e.tensor_scalar_mul(nxt[f][:, tsl],
                                                 nxt[f][:, tsl], sgn),
                         waits=(tw,), inc=tw)
                    p.on("vector",
                         lambda e, f=f, tsl=tsl, cur=cur, nxt=nxt:
                             e.tensor_tensor(
                                 nxt[f][:, tsl], nxt[f][:, tsl],
                                 cur[f][:, tsl], ALU.add),
                         waits=(tw,), inc=tw)
                    if cyc < ncyc - 1:
                        p.on("vector",
                             lambda e, f=f, tsl=tsl, nxt=nxt: e.tensor_copy(
                                 sbf[f][:, tsl], nxt[f][:, tsl]),
                             waits=(tw,), inc=ev)
                    else:
                        p.on("vector", lambda e: e.nop(), waits=(tw,),
                             inc=ev)
            cur, nxt = nxt, cur
        if dbg_d is not None:
            p.dma("sync", lambda hsl=hsl: dbg_d[3:4, hsl],
                  lambda cur=cur: (cur[0][0:1, :]), st, waits=(ev,))
        for f in range(NF):
            p.on("vector",
                 lambda e, f=f, cur=cur: e.tensor_tensor(
                     cur[f][:, :], cur[f][:, :], x1h[f][:, :], ALU.add),
                 waits=(ev, hld), inc=ev)
            p.dma("sync",
                  lambda f=f, hsl=hsl: x2_d[f * 128:(f + 1) * 128, hsl],
                  lambda f=f, cur=cur: (cur[f][:, :]), st, waits=(ev,))
    p.on("gpsimd", lambda e: e.nop(), waits=(st,))
    return p.emit()




def build_E():
    p = Prog()
    nc = p.nc
    ld = p.ctr("ld")
    C = load_consts(p, ld)
    _, x_sb, _ = emit_load_xT(p, "xT", ctr=ld)
    w_d = nc.dram_tensor("Wkqv", [16, D, 2112], F32, kind="ExternalInput")
    b_d = nc.dram_tensor("bkqv_cols", [16, 128, 17], F32, kind="ExternalInput")
    K_o = nc.dram_tensor("K_o", [16, D, CHUNK], F32, kind="ExternalOutput")
    Q_o = nc.dram_tensor("Q_o", [16, D, CHUNK], F32, kind="ExternalOutput")
    V_o = nc.dram_tensor("V_o", [16, 64, CHUNK], F32, kind="ExternalOutput")

    xn_sb = [p.sbuf(f"xn{f}", [128, CHUNK], BF16) for f in range(NF)]
    ln_done = p.ctr("ln")
    emit_ln(p, C, x_sb, xn_sb, ld, ln_done)

    slab = [p.sbuf(f"wk{k}", [128, 2112], BF16) for k in range(NF)]
    bias_sb = p.sbuf("bias", [128, 17], F32)
    y_ps = [p.psum(f"yp{t}", [128, TB], F32) for t in range(NTB)]
    out_f = p.sbuf("out_f", [128, TB], F32)
    wld = p.ctr("wld")
    mm = p.ctr("mm")
    ev = p.ctr("ev")
    st = p.ctr("st")

    for h in range(16):
        for k in range(NF):
            p.on("gpsimd",
                 lambda e, h=h, k=k: e.dma_start(
                     slab[k][:, :], w_d[h, k * 128:(k + 1) * 128, :]),
                 waits=(mm,), inc=wld, inc_k=16)
        p.on("scalar",
             lambda e, h=h: e.dma_start(bias_sb[:, :], b_d[h, :, :]),
             waits=(ev,), inc=wld, inc_k=16)
        wld_now = (wld.sem, wld.v)
        for m in range(17):
            mw = 64 if m == 16 else 128
            csl = slice(m * 128, m * 128 + mw)
            for k in range(NF):
                for tb in range(NTB):
                    sl = slice(tb * TB, (tb + 1) * TB)
                    waits = [ln_done, wld_now]
                    if k == 0:
                        waits.append(ev)
                    p.on("tensor",
                         lambda e, m=m, k=k, tb=tb, sl=sl, csl=csl, mw=mw:
                             e.matmul(y_ps[tb][0:mw, :], slab[k][:, csl],
                                      xn_sb[k][:, sl],
                                      start=(k == 0), stop=(k == NF - 1)),
                         waits=waits, inc=(mm if k == NF - 1 else None))
            for tb in range(NTB):
                sl = slice(tb * TB, (tb + 1) * TB)
                if m < 8:
                    dst = lambda h=h, m=m, sl=sl: K_o[h, m * 128:(m + 1) * 128, sl]
                elif m < 16:
                    dst = lambda h=h, m=m, sl=sl: Q_o[h, (m - 8) * 128:(m - 7) * 128, sl]
                else:
                    dst = lambda h=h, sl=sl: V_o[h, :, sl]
                mw = 64 if m == 16 else 128
                p.on("scalar",
                     lambda e, tb=tb, m=m, mw=mw: e.activation(
                         out_f[0:mw, :], y_ps[tb][0:mw, :], AF.Identity,
                         bias=bias_sb[0:mw, m:m + 1]),
                     waits=(mm, st), inc=ev)
                p.dma("sync", dst,
                      lambda mw=mw: out_f[0:mw, :], st, waits=(ev,))
    p.on("gpsimd", lambda e: e.nop(), waits=(st,))
    return p.emit()



NQB = SEQ // TB
NKT = SEQ // 128


def build_F(npairs=4):
    p = Prog()
    nc = p.nc
    ld = p.ctr("ld")
    C = load_consts(p, ld)
    Q_d = nc.dram_tensor("Q4", [npairs, D, SEQ], F32, kind="ExternalInput")
    K_d = nc.dram_tensor("K4", [npairs, D, SEQ], F32, kind="ExternalInput")
    V_d = nc.dram_tensor("V4", [npairs, SEQ, 65], F32, kind="ExternalInput")
    M_d = nc.dram_tensor("mask", [128, 896], F32, kind="ExternalInput")
    sa_d = nc.dram_tensor("sa4", [npairs, 64, SEQ], F32, kind="ExternalOutput")

    msk = p.sbuf("msk", [128, 896], BF16)
    p.dma("gpsimd", lambda: msk[:, :], lambda: M_d[:, :], ld)
    qt = [p.sbuf(f"qt{dc}", [128, 4 * TB], BF16) for dc in range(NF)]
    kt_b = [p.sbuf(f"ktb{i}", [128, D], BF16) for i in range(2)]
    vt_b = [p.sbuf(f"vtb{i}", [128, 65], BF16) for i in range(2)]
    pt = [p.sbuf(f"pt{i}", [128, TB], BF16) for i in range(4)]
    s_ps = [p.psum(f"sps{i}", [128, TB], F32) for i in range(4)]
    o_ps = [p.psum(f"ops{i}", [128, TB], F32) for i in range(4)]
    cbias = p.sbuf("cbias", [128, 1], F32)
    srow = p.sbuf("srow", [128, 1], F32)
    srow_b = p.sbuf("srow_b", [128, 1], BF16)
    sc1 = p.sbuf("sc1", [1, 1], F32)
    rrow = p.sbuf("rrow", [1, TB], F32)
    bc_sb = p.sbuf("bc_sb", [64, TB], F32)
    sa_sb = p.sbuf("sa_sb", [64, TB], F32)

    for pr in range(npairs):
        qld = p.ctr("qld")
        kld = p.ctr("kld")
        mm = p.ctr("mm")
        ev = p.ctr("ev")
        st = p.ctr("st")

        def load_kt(i, kt, extra=()):
            p.on("gpsimd",
                 lambda e, i=i, kt=kt, pr=pr: e.dma_start(
                     kt_b[i][:, :].rearrange("p (dc j) -> p dc j", dc=NF),
                     K_d[pr, :, kt * 128:(kt + 1) * 128]
                     .rearrange("(dc p) j -> p dc j", p=128)),
                 waits=extra, inc=kld, inc_k=16)
            p.on("gpsimd",
                 lambda e, i=i, kt=kt, pr=pr: e.dma_start(
                     vt_b[i][:, :], V_d[pr, kt * 128:(kt + 1) * 128, :]),
                 waits=extra, inc=kld, inc_k=16)

        def load_q(qbg, extra=()):
            for dc in range(NF):
                p.on("gpsimd",
                     lambda e, dc=dc, qbg=qbg, pr=pr: e.dma_start(
                         qt[dc][:, :],
                         Q_d[pr, dc * 128:(dc + 1) * 128,
                             qbg * 4 * TB:(qbg + 1) * 4 * TB]),
                     waits=extra, inc=qld, inc_k=16)

        load_kt(0, 0, extra=(ev,))
        load_q(0, extra=(ev,))
        for dc in range(NF):
            p.on("tensor",
                 lambda e, dc=dc: e.matmul(
                     s_ps[0][:, :], kt_b[0][:, dc * 128:(dc + 1) * 128],
                     qt[dc][:, 0:TB], start=(dc == 0), stop=(dc == NF - 1)),
                 waits=(kld, qld), inc=(mm if dc == NF - 1 else None))
        p.on("vector", lambda e: e.reduce_max(srow[:, :], s_ps[0][:, :], AX.X),
             waits=(mm,), inc=ev)
        p.on("vector", lambda e: e.tensor_copy(srow_b[:, :], srow[:, :]),
             waits=(ev,), inc=ev)
        p.on("tensor",
             lambda e: e.transpose(o_ps[0][0:1, :].bitcast(BF16)[:, 0:128],
                                   srow_b[:, :], C["ident"]),
             waits=(ev,), inc=mm)
        p.on("vector",
             lambda e: e.reduce_max(sc1[:, :],
                                    o_ps[0][0:1, :].bitcast(BF16)[:, 0:128],
                                    AX.X),
             waits=(mm,), inc=ev)
        p.on("vector",
             lambda e: e.tensor_scalar(sc1[:, :], sc1[:, :],
                                       -1.0 / 32.0, -2.0, ALU.mult, ALU.add),
             waits=(ev,), inc=ev)
        p.on("tensor",
             lambda e: e.matmul(s_ps[1][:, 0:1], C["ones_row"], sc1[:, :],
                                start=True, stop=True),
             waits=(ev,), inc=mm)
        p.on("vector", lambda e: e.tensor_copy(cbias[:, :], s_ps[1][:, 0:1]),
             waits=(mm,), inc=ev)

        for qbg in range(4):
            load_q(qbg, extra=(mm, ev))
            for kt in range(16 * (qbg + 1)):
                load_kt(kt % 2, kt, extra=((mm, ev) if kt >= 2 else (mm,)))
                kld_now = (kld.sem, kld.v)
                for qbl in range(4):
                    qq = qbg * 4 + qbl
                    if kt >= 4 * (qq + 1):
                        continue
                    for dc in range(NF):
                        waits = [kld_now, qld]
                        if dc == 0:
                            waits.append(ev)
                        p.on("tensor",
                             lambda e, kt=kt, qbl=qbl, dc=dc: e.matmul(
                                 s_ps[qbl][:, :],
                                 kt_b[kt % 2][:, dc * 128:(dc + 1) * 128],
                                 qt[dc][:, qbl * TB:(qbl + 1) * TB],
                                 start=(dc == 0), stop=(dc == NF - 1)),
                             waits=waits, inc=(mm if dc == NF - 1 else None))
                    p.on("scalar",
                         lambda e, qbl=qbl: e.activation(
                             pt[qbl][:, :], s_ps[qbl][:, :], AF.Exp,
                             bias=cbias[:, :], scale=np.float32(1.0 / 32.0)),
                         waits=(mm,), inc=ev)
                    if kt >= 4 * qq:
                        off = 384 - 128 * (kt - 4 * qq)
                        p.on("vector",
                             lambda e, qbl=qbl, off=off: e.tensor_tensor(
                                 pt[qbl][:, :], pt[qbl][:, :],
                                 msk[:, off:off + TB], ALU.mult),
                             waits=(ev, ld), inc=ev)
                    p.on("tensor",
                         lambda e, kt=kt, qbl=qbl, qq=qq: e.matmul(
                             o_ps[qbl][0:65, :], vt_b[kt % 2][:, :],
                             pt[qbl][:, :], start=(kt == 0),
                             stop=(kt == 4 * qq + 3)),
                         waits=(ev,), inc=mm)
            for qbl in range(4):
                qq = qbg * 4 + qbl
                p.on("vector",
                     lambda e, qbl=qbl: e.reciprocal(rrow[:, :],
                                                     o_ps[qbl][64:65, :]),
                     waits=(mm,), inc=ev)
                p.on("tensor",
                     lambda e, qbl=qbl: e.matmul(
                         s_ps[qbl][0:64, :], C["ones_row"][:, 0:64],
                         rrow[:, :], start=True, stop=True),
                     waits=(ev,), inc=mm)
                p.on("vector",
                     lambda e, qbl=qbl: e.tensor_copy(bc_sb[:, :],
                                                      s_ps[qbl][0:64, :]),
                     waits=(mm,), inc=ev)
                p.on("vector",
                     lambda e, qbl=qbl: e.tensor_tensor(
                         sa_sb[:, :], o_ps[qbl][0:64, :], bc_sb[:, :],
                         ALU.mult),
                     waits=(ev, st), inc=ev)
                p.dma("sync",
                      lambda pr=pr, qq=qq: sa_d[pr, :, qq * TB:(qq + 1) * TB],
                      lambda: sa_sb[:, :], st, waits=(ev,))
        p.on("gpsimd", lambda e: e.nop(), waits=(st,))
    p.on("gpsimd", lambda e: e.nop(), waits=(p.ctrs[-1],))
    return p.emit()




def build_P():
    p = Prog()
    nc = p.nc
    ld = p.ctr("ld")
    C = load_consts(p, ld)
    _, x_sb, _ = emit_load_xT(p, "xT", ctr=ld)
    a_d = nc.dram_tensor("aT", [D, CHUNK], F32, kind="ExternalInput")
    w_d = nc.dram_tensor("w", [D, D], F32, kind="ExternalInput")
    b_d = nc.dram_tensor("b_cols", [128, NF], F32, kind="ExternalInput")
    xo_d = nc.dram_tensor("xoT", [D, CHUNK], F32, kind="ExternalOutput")
    s_d = nc.dram_tensor("sT", [D, CHUNK], F32, kind="ExternalOutput")
    sums_d = nc.dram_tensor("sums", [128, NF], F32, kind="ExternalOutput")

    bias_sb = p.sbuf("bias", [128, NF], F32)
    p.dma("scalar", lambda: bias_sb[:, :], lambda: b_d[:, :], ld)
    a_sb = [p.sbuf(f"a{f}", [128, CHUNK], BF16) for f in range(NF)]
    for f in range(NF):
        p.dma("gpsimd", lambda f=f: a_sb[f][:, :],
              lambda f=f: a_d[f * 128:(f + 1) * 128, :], ld)
    pslab = emit_wslabs(p, w_d, NF, D, ld, "pw")

    ps_stack = ExitStack()
    y_ps = [p.psum(f"yp{t}", [128, TB], F32, stack=ps_stack)
            for t in range(NTB)]
    mm = p.ctr("mm")
    ev = p.ctr("ev")
    st = p.ctr("st")

    def evac(m, tb):
        sl = slice(tb * TB, (tb + 1) * TB)
        p.on("vector",
             lambda e, m=m, sl=sl, tb=tb: e.scalar_tensor_tensor(
                 x_sb[m][:, sl], y_ps[tb][:, :], bias_sb[:, m:m + 1],
                 x_sb[m][:, sl], ALU.add, ALU.add),
             waits=(mm,), inc=ev)
        p.dma("sync", lambda m=m, sl=sl: xo_d[m * 128:(m + 1) * 128, sl],
              lambda m=m, sl=sl: x_sb[m][:, sl], st, waits=(ev,))

    emit_mm_chain(
        p, lambda k, m: pslab[k][:, m * 128:(m + 1) * 128],
        lambda k, tb: a_sb[k][:, tb * TB:(tb + 1) * TB],
        y_ps, NF, NF, NTB, mm, ev, [ld], evac)

    ps_stack.close()
    sums_sb = p.sbuf("sums_sb", [128, NF], F32)
    ln2 = p.ctr("ln2")
    emit_ln_dram(p, C, x_sb, s_d, sums_sb, ev, ln2, st)
    p.dma("sync", lambda: sums_d[:, :], lambda: sums_sb[:, :], st,
          waits=(ln2,))
    p.on("gpsimd", lambda e: e.nop(), waits=(st,))
    return p.emit()




def build_H():
    p = Prog()
    nc = p.nc
    ld = p.ctr("ld")
    C = load_consts(p, ld)
    _, x_sb, _ = emit_load_xT(p, "xT", ctr=ld)
    wq_d = nc.dram_tensor("Wq", [D, D], F32, kind="ExternalInput")
    wk_d = nc.dram_tensor("Wk", [D, D], F32, kind="ExternalInput")
    wv_d = nc.dram_tensor("Wv", [D, D], F32, kind="ExternalInput")
    bq_d = nc.dram_tensor("bq_cols", [128, NF], F32, kind="ExternalInput")
    bk_d = nc.dram_tensor("bk_cols", [128, NF], F32, kind="ExternalInput")
    bv_d = nc.dram_tensor("bv_cols", [128, NF], F32, kind="ExternalInput")
    q_o = nc.dram_tensor("qT", [D, CHUNK], F32, kind="ExternalOutput")
    k_o = nc.dram_tensor("kT", [D, CHUNK], F32, kind="ExternalOutput")
    v_o = nc.dram_tensor("vT", [D, CHUNK], F32, kind="ExternalOutput")

    bias_sb = p.sbuf("bias", [128, 3 * NF], F32)
    p.dma("scalar", lambda: bias_sb[:, 0:NF], lambda: bq_d[:, :], ld)
    p.dma("scalar", lambda: bias_sb[:, NF:2 * NF], lambda: bk_d[:, :], ld)
    p.dma("scalar", lambda: bias_sb[:, 2 * NF:3 * NF], lambda: bv_d[:, :], ld)
    xn_sb = [p.sbuf(f"xn{f}", [128, CHUNK], BF16) for f in range(NF)]
    ln_done = p.ctr("ln")
    emit_ln(p, C, x_sb, xn_sb, ld, ln_done)

    wld = p.ctr("wld")
    qs = emit_wslabs(p, wq_d, NF, D, wld, "wq")
    ks = emit_wslabs(p, wk_d, NF, D, wld, "wk")
    vs = emit_wslabs(p, wv_d, NF, D, wld, "wv")
    slabs = {0: qs, 1: ks, 2: vs}
    outs = {0: q_o, 1: k_o, 2: v_o}
    y_ps = [p.psum(f"yp{t}", [128, TB], F32) for t in range(NTB)]
    out_f = p.sbuf("out_f", [128, TB], F32)
    mm = p.ctr("mm")
    ev = p.ctr("ev")
    st = p.ctr("st")
    for w in range(3):
        for m in range(NF):
            for k in range(NF):
                for tb in range(NTB):
                    sl = slice(tb * TB, (tb + 1) * TB)
                    waits = [ln_done, (wld.sem, 16 * NF * (w + 1))]
                    if k == 0:
                        waits.append(ev)
                    p.on("tensor",
                         lambda e, w=w, m=m, k=k, tb=tb, sl=sl: e.matmul(
                             y_ps[tb][:, :],
                             slabs[w][k][:, m * 128:(m + 1) * 128],
                             xn_sb[k][:, sl],
                             start=(k == 0), stop=(k == NF - 1)),
                         waits=waits, inc=(mm if k == NF - 1 else None))
            for tb in range(NTB):
                sl = slice(tb * TB, (tb + 1) * TB)
                p.on("scalar",
                     lambda e, w=w, m=m, tb=tb: e.activation(
                         out_f[:, :], y_ps[tb][:, :], AF.Identity,
                         bias=bias_sb[:, w * NF + m:w * NF + m + 1]),
                     waits=(mm, st), inc=ev)
                p.dma("sync",
                      lambda w=w, m=m, sl=sl: outs[w][m * 128:(m + 1) * 128, sl],
                      lambda: out_f[:, :], st, waits=(ev,))
    p.on("gpsimd", lambda e: e.nop(), waits=(st,))
    return p.emit()



def build_I():
    p = Prog()
    nc = p.nc
    ld = p.ctr("ld")
    C = load_consts(p, ld)
    q_d = nc.dram_tensor("qT", [D, CHUNK], F32, kind="ExternalInput")
    k_d = nc.dram_tensor("kT_full", [D, SEQ], F32, kind="ExternalInput")
    v_d = nc.dram_tensor("v_aug", [16, SEQ, 65], F32, kind="ExternalInput")
    ao_d = nc.dram_tensor("aoT", [D, CHUNK], F32, kind="ExternalOutput")

    q_sb = [p.sbuf(f"q{f}", [128, CHUNK], BF16) for f in range(NF)]
    for f in range(NF):
        p.dma("gpsimd", lambda f=f: q_sb[f][:, :],
              lambda f=f: q_d[f * 128:(f + 1) * 128, :], ld)
    kft = p.sbuf("kft", [128, SEQ], BF16)
    vt = p.sbuf("vt", [128, NKT * 65], BF16)
    pt = [p.sbuf(f"pt{i}", [128, TB], BF16) for i in range(4)]
    s_ps = [p.psum(f"sps{i}", [128, TB], F32) for i in range(4)]
    o_ps = [p.psum(f"ops{i}", [128, TB], F32) for i in range(4)]
    cbias = p.sbuf("cbias", [128, 1], F32)
    srow = p.sbuf("srow", [128, 1], F32)
    srow_b = p.sbuf("srow_b", [128, 1], BF16)
    sc1 = p.sbuf("sc1", [1, 1], F32)
    rrow = p.sbuf("rrow", [1, TB], F32)
    bc_sb = p.sbuf("bc_sb", [64, TB], F32)
    ao_sb = [p.sbuf(f"ao{f}", [128, CHUNK], F32) for f in range(NF)]

    kld = p.ctr("kld")
    mm = p.ctr("mm")
    ev = p.ctr("ev")
    st = p.ctr("st")
    for h in range(16):
        fo, po = h // 2, (h % 2) * 64
        if h % 2 == 0:
            p.on("gpsimd",
                 lambda e, fo=fo: e.dma_start(
                     kft[:, :], k_d[fo * 128:(fo + 1) * 128, :]),
                 waits=(mm, ev), inc=kld, inc_k=16)
        p.on("gpsimd",
             lambda e, h=h: e.dma_start(
                 vt[:, :].rearrange("p (kt j) -> p kt j", j=65),
                 v_d[h, :, :].rearrange("(kt p) j -> p kt j", p=128)),
             waits=(mm, ev), inc=kld, inc_k=16)
        kld_now = (kld.sem, kld.v)
        p.on("tensor",
             lambda e, po=po, fo=fo: e.matmul(
                 s_ps[0][:, :], kft[po:po + 64, 0:128],
                 q_sb[fo][po:po + 64, 0:TB], start=True, stop=True),
             waits=(kld_now, ld, ev), inc=mm)
        p.on("vector", lambda e: e.reduce_max(srow[:, :], s_ps[0][:, :], AX.X),
             waits=(mm,), inc=ev)
        p.on("vector", lambda e: e.tensor_copy(srow_b[:, :], srow[:, :]),
             waits=(ev,), inc=ev)
        p.on("tensor",
             lambda e: e.transpose(o_ps[0][0:1, :].bitcast(BF16)[:, 0:128],
                                   srow_b[:, :], C["ident"]),
             waits=(ev,), inc=mm)
        p.on("vector",
             lambda e: e.reduce_max(sc1[:, :],
                                    o_ps[0][0:1, :].bitcast(BF16)[:, 0:128],
                                    AX.X),
             waits=(mm,), inc=ev)
        p.on("vector",
             lambda e: e.tensor_scalar(sc1[:, :], sc1[:, :],
                                       -1.0 / 8.0, -2.0, ALU.mult, ALU.add),
             waits=(ev,), inc=ev)
        p.on("tensor",
             lambda e: e.matmul(s_ps[1][:, 0:1], C["ones_row"], sc1[:, :],
                                start=True, stop=True),
             waits=(ev,), inc=mm)
        p.on("vector", lambda e: e.tensor_copy(cbias[:, :], s_ps[1][:, 0:1]),
             waits=(mm,), inc=ev)
        fq = fo
        for kt in range(NKT):
            for qbl in range(4):
                waits = [kld_now, ld]
                waits.append(ev)
                p.on("tensor",
                     lambda e, kt=kt, qbl=qbl, fq=fq, po=po: e.matmul(
                         s_ps[qbl][:, :],
                         kft[po:po + 64, kt * 128:(kt + 1) * 128],
                         q_sb[fq][po:po + 64, qbl * TB:(qbl + 1) * TB],
                         start=True, stop=True),
                     waits=waits, inc=mm)
                p.on("scalar",
                     lambda e, qbl=qbl: e.activation(
                         pt[qbl][:, :], s_ps[qbl][:, :], AF.Exp,
                         bias=cbias[:, :], scale=np.float32(1.0 / 8.0)),
                     waits=(mm,), inc=ev)
                p.on("tensor",
                     lambda e, kt=kt, qbl=qbl: e.matmul(
                         o_ps[qbl][0:65, :],
                         vt[:, kt * 65:(kt + 1) * 65], pt[qbl][:, :],
                         start=(kt == 0), stop=(kt == NKT - 1)),
                     waits=(ev,), inc=mm)
        for qbl in range(4):
            p.on("vector",
                 lambda e, qbl=qbl: e.reciprocal(rrow[:, :],
                                                 o_ps[qbl][64:65, :]),
                 waits=(mm,), inc=ev)
            p.on("tensor",
                 lambda e, qbl=qbl: e.matmul(
                     s_ps[qbl][0:64, :], C["ones_row"][:, 0:64],
                     rrow[:, :], start=True, stop=True),
                 waits=(ev,), inc=mm)
            p.on("vector",
                 lambda e, qbl=qbl: e.tensor_copy(bc_sb[:, :],
                                                  s_ps[qbl][0:64, :]),
                 waits=(mm,), inc=ev)
            p.on("vector",
                 lambda e, qbl=qbl, fo=fo, po=po: e.tensor_tensor(
                     ao_sb[fo][po:po + 64, qbl * TB:(qbl + 1) * TB],
                     o_ps[qbl][0:64, :], bc_sb[:, :], ALU.mult),
                 waits=(ev,), inc=ev)
    for f in range(NF):
        p.dma("sync", lambda f=f: ao_d[f * 128:(f + 1) * 128, :],
              lambda f=f: ao_sb[f][:, :], st, waits=(ev,))
    p.on("gpsimd", lambda e: e.nop(), waits=(st,))
    return p.emit()



def build_R():
    p = Prog()
    nc = p.nc
    ld = p.ctr("ld")
    C = load_consts(p, ld)
    _, x_sb, _ = emit_load_xT(p, "xT", ctr=ld)
    s_d = nc.dram_tensor("sT", [D, CHUNK], F32, kind="ExternalInput")
    g2_d = nc.dram_tensor("g2lhs", [128, 112], F32, kind="ExternalInput")
    g1_d = nc.dram_tensor("g1rhs", [256, 32], F32, kind="ExternalInput")
    br_d = nc.dram_tensor("br_cols", [128, 4], F32, kind="ExternalInput")
    bi_d = nc.dram_tensor("bi_cols", [128, 4], F32, kind="ExternalInput")
    xo_d = nc.dram_tensor("xoT", [D, CHUNK], F32, kind="ExternalOutput")
    so_d = nc.dram_tensor("sT_o", [D, CHUNK], F32, kind="ExternalOutput")

    s_sb = [p.sbuf(f"s{f}", [128, CHUNK], BF16) for f in range(NF)]
    for f in range(NF):
        p.dma("gpsimd", lambda f=f: s_sb[f][:, :],
              lambda f=f: s_d[f * 128:(f + 1) * 128, :], ld)
    bias_sb = p.sbuf("bias", [128, 8], F32)
    p.dma("scalar", lambda: bias_sb[:, 0:4], lambda: br_d[:, :], ld)
    p.dma("scalar", lambda: bias_sb[:, 4:8], lambda: bi_d[:, :], ld)
    g2row = p.sbuf("g2row", [1, 128 * 112], BF16)
    g1row = p.sbuf("g1row", [1, 256 * 32], BF16)
    p.dma("gpsimd", lambda: g2row[:, :].rearrange("o (r c) -> o r c", c=112),
          lambda: g2_d[:, :].rearrange("(o r) c -> o r c", o=1), ld)
    p.dma("gpsimd", lambda: g1row[:, :].rearrange("o (r c) -> o r c", c=32),
          lambda: g1_d[:, :].rearrange("(o r) c -> o r c", o=1), ld)

    w_sb = [[p.sbuf(f"w{ri}_{t}", [128, 512], BF16) for t in range(4)]
            for ri in range(2)]
    wt_sb = [[p.sbuf(f"wt{ri}_{t}", [128, 512], BF16) for t in range(4)]
             for ri in range(2)]
    wtn_sb = [p.sbuf(f"wtn{t}", [128, 512], BF16) for t in range(4)]
    w_ps = p.psum("w_ps", [128, TB], F32)
    t_ps = p.psum("t_ps", [128, 128], BF16)
    mm = p.ctr("mm")
    ev = p.ctr("ev")
    st = p.ctr("st")
    for ri in range(2):
        for t in range(4):
            for wnd in range(2):
                for g in range(16):
                    for c in range(4):
                        for par in range(4):
                            b = 8 * t + 4 * wnd + par
                            li = (ri * 4 + c) * 16 + g
                            off = li * 112 + 48 - par * 16
                            gi = (ri * 4 + c) * 32 + b
                            last = (wnd == 1 and g == 15 and c == 3
                                    and par == 3)
                            p.on("tensor",
                                 lambda e, wnd=wnd, g=g, off=off, gi=gi,
                                 first=(c == 0 and par == 0): e.matmul(
                                     w_ps[64 * wnd:64 * wnd + 64,
                                          g * 32:(g + 1) * 32],
                                     g2row[:, off:off + 64],
                                     g1row[:, gi * 32:gi * 32 + 32],
                                     start=first, stop=False,
                                     skip_group_check=True),
                                 waits=([ld, ev] if (g == 0 and c == 0 and par == 0) else [ld]),
                                 inc=(mm if last else None))
            for g in range(16):
                p.on("vector",
                     lambda e, ri=ri, t=t, g=g: e.tensor_copy(
                         w_sb[ri][t][:, :].rearrange(
                             "p (d g2) -> p g2 d", g2=16)[:, g, :],
                         w_ps[:, g * 32:(g + 1) * 32]),
                     waits=(mm,), inc=ev)
    for ri in range(2):
        for t in range(4):
            for u in range(4):
                p.on("tensor",
                     lambda e, ri=ri, t=t, u=u: e.transpose(
                         t_ps[:, :], w_sb[ri][t][:, u * 128:(u + 1) * 128],
                         C["ident"]),
                     waits=(ev,), inc=mm)
                p.on("vector",
                     lambda e, ri=ri, t=t, u=u: e.tensor_copy(
                         wt_sb[ri][u][:, t * 128:(t + 1) * 128], t_ps[:, :]),
                     waits=(mm,), inc=ev)
    for u in range(4):
        p.on("vector",
             lambda e, u=u: e.tensor_scalar_mul(wtn_sb[u][:, :],
                                                wt_sb[1][u][:, :], -1.0),
             waits=(ev,), inc=ev)

    y_ps = [p.psum(f"yp{t}", [128, TB], F32) for t in range(2)]
    mm2 = p.ctr("mm2")
    ev2 = p.ctr("ev2")
    for half in range(2):
        for m in range(4):
            for tb in range(NTB):
                sl = slice(tb * TB, (tb + 1) * TB)
                for k in range(8):
                    xk = k
                    if half == 0:
                        wmat = wt_sb[0][k][:, m * 128:(m + 1) * 128] if k < 4                             else wtn_sb[k - 4][:, m * 128:(m + 1) * 128]
                    else:
                        wmat = wt_sb[1][k][:, m * 128:(m + 1) * 128] if k < 4                             else wt_sb[0][k - 4][:, m * 128:(m + 1) * 128]
                    waits = [ld, ev]
                    if k == 0:
                        waits.append(ev2)
                    p.on("tensor",
                         lambda e, wmat=wmat, xk=xk, tb=tb, sl=sl, k=k:
                             e.matmul(y_ps[tb % 2][:, :], wmat,
                                      s_sb[xk][:, sl],
                                      start=(k == 0), stop=(k == 7)),
                         waits=waits, inc=(mm2 if k == 7 else None))
                fo = half * 4 + m
                p.on("vector",
                     lambda e, fo=fo, sl=sl, tb=tb, half=half, m=m:
                         e.scalar_tensor_tensor(
                             x_sb[fo][:, sl], y_ps[tb % 2][:, :],
                             bias_sb[:, half * 4 + m:half * 4 + m + 1],
                             x_sb[fo][:, sl], ALU.add, ALU.add),
                     waits=(mm2,), inc=ev2)
                p.dma("sync",
                      lambda fo=fo, sl=sl: xo_d[fo * 128:(fo + 1) * 128, sl],
                      lambda fo=fo, sl=sl: x_sb[fo][:, sl], st, waits=(ev2,))
    sums_sb = p.sbuf("sums_sb", [128, NF], F32)
    ln2 = p.ctr("ln2")
    emit_ln_dram(p, C, x_sb, so_d, sums_sb, ev2, ln2, st)
    p.on("gpsimd", lambda e: e.nop(), waits=(st,))
    return p.emit()



def build_K1():
    p = Prog()
    nc = p.nc
    ld = p.ctr("ld")
    C = load_consts(p, ld)
    s_d = nc.dram_tensor("sT", [D, CHUNK], F32, kind="ExternalInput")
    w_d = nc.dram_tensor("w_qkv", [D, 3 * D], F32, kind="ExternalInput")
    q_o = nc.dram_tensor("phqT", [D, CHUNK], F32, kind="ExternalOutput")
    k_o = nc.dram_tensor("phkT", [D, CHUNK], F32, kind="ExternalOutput")
    v_o = nc.dram_tensor("vT", [D, CHUNK], F32, kind="ExternalOutput")
    outs = {0: q_o, 1: k_o, 2: v_o}

    s_sb = [p.sbuf(f"s{f}", [128, CHUNK], BF16) for f in range(NF)]
    for f in range(NF):
        p.dma("gpsimd", lambda f=f: s_sb[f][:, :],
              lambda f=f: s_d[f * 128:(f + 1) * 128, :], ld)
    wslab = emit_wslabs(p, w_d, NF, 3 * D, ld, "wsl")
    y_ps = [p.psum(f"yp{t}", [128, TB], F32) for t in range(NTB)]
    t1 = p.sbuf("t1", [128, TB], F32)
    t2 = p.sbuf("t2", [128, TB], F32)
    out_f = p.sbuf("out_f", [128, TB], F32)
    mm = p.ctr("mm")
    ev = p.ctr("ev")
    st = p.ctr("st")
    for m in range(24):
        s_part = m // NF
        fo = m % NF
        for k in range(NF):
            for tb in range(NTB):
                sl = slice(tb * TB, (tb + 1) * TB)
                waits = [ld]
                if k == 0:
                    waits.append(ev)
                p.on("tensor",
                     lambda e, m=m, k=k, tb=tb, sl=sl: e.matmul(
                         y_ps[tb][:, :], wslab[k][:, m * 128:(m + 1) * 128],
                         s_sb[k][:, sl], start=(k == 0), stop=(k == NF - 1)),
                     waits=waits, inc=(mm if k == NF - 1 else None))
        for tb in range(NTB):
            sl = slice(tb * TB, (tb + 1) * TB)
            if s_part == 2:
                p.on("scalar",
                     lambda e, tb=tb: e.activation(out_f[:, :],
                                                   y_ps[tb][:, :],
                                                   AF.Identity),
                     waits=(mm, st), inc=ev)
            else:
                p.on("vector",
                     lambda e, tb=tb: e.tensor_scalar_min(
                         t1[:, :], y_ps[tb][:, :], 0.0),
                     waits=(mm, ev, st), inc=ev)
                p.on("scalar",
                     lambda e: e.activation(t2[:, :], t1[:, :], AF.Exp),
                     waits=(ev,), inc=ev)
                p.on("vector",
                     lambda e, tb=tb: e.tensor_scalar_max(
                         t1[:, :], y_ps[tb][:, :], 0.0),
                     waits=(ev,), inc=ev)
                p.on("vector",
                     lambda e: e.tensor_tensor(out_f[:, :], t1[:, :],
                                               t2[:, :], ALU.add),
                     waits=(ev,), inc=ev)
            p.dma("sync",
                  lambda fo=fo, sl=sl, s_part=s_part:
                      outs[s_part][fo * 128:(fo + 1) * 128, sl],
                  lambda: out_f[:, :], st, waits=(ev,))
    p.on("gpsimd", lambda e: e.nop(), waits=(st,))
    return p.emit()



def build_L1b():
    p = Prog()
    nc = p.nc
    ld = p.ctr("ld")
    C = load_consts(p, ld)
    q_d = nc.dram_tensor("phqT", [D, CHUNK], F32, kind="ExternalInput")
    ka_d = nc.dram_tensor("phk_aug", [16, SEQ, 65], F32, kind="ExternalInput")
    vk_d = nc.dram_tensor("vphk", [16, SEQ, 128], F32, kind="ExternalInput")
    ao_d = nc.dram_tensor("aoT", [D, CHUNK], F32, kind="ExternalOutput")

    q_sb = [p.sbuf(f"q{f}", [128, CHUNK], BF16) for f in range(NF)]
    for f in range(NF):
        p.dma("gpsimd", lambda f=f: q_sb[f][:, :],
              lambda f=f: q_d[f * 128:(f + 1) * 128, :], ld)
    ka_t = [p.sbuf(f"ka{i}", [128, 65], BF16) for i in range(2)]
    vk_t = [p.sbuf(f"vk{i}", [128, 128], BF16) for i in range(2)]
    kvz_ps = p.psum("kvz", [65, 128], F32)
    z_ps = p.psum("z_ps", [64, 1], F32)
    kv_bf = p.sbuf("kv_bf", [128, 64], BF16)
    z_bf = p.sbuf("z_bf", [128, 1], BF16)
    n_ps = p.psum("n_ps", [64, TB], F32)
    d_ps = p.psum("d_ps", [32, TB], F32)
    b_ps = p.psum("b_ps", [64, TB], F32)
    drow = p.sbuf("drow", [1, TB], F32)
    bc_sb = p.sbuf("bc_sb", [64, TB], F32)
    ao_sb = [p.sbuf(f"ao{f}", [128, CHUNK], F32) for f in range(NF)]
    kld = p.ctr("kld")
    mm = p.ctr("mm")
    ev = p.ctr("ev")
    st = p.ctr("st")
    for h in range(16):
        fo, po = h // 2, (h % 2) * 64
        marks = []
        for tt in range(NKT):
            waits = [] if tt < 2 else [(mm.sem, marks[tt - 2])]
            p.on("gpsimd",
                 lambda e, h=h, tt=tt: e.dma_start(
                     ka_t[tt % 2][:, :], ka_d[h, tt * 128:(tt + 1) * 128, :]),
                 waits=waits, inc=kld, inc_k=16)
            p.on("gpsimd",
                 lambda e, h=h, tt=tt: e.dma_start(
                     vk_t[tt % 2][:, :], vk_d[h, tt * 128:(tt + 1) * 128, :]),
                 waits=waits, inc=kld, inc_k=16)
            p.on("tensor",
                 lambda e, tt=tt: e.matmul(
                     kvz_ps[:, :], ka_t[tt % 2][:, :], vk_t[tt % 2][:, :],
                     start=(tt == 0), stop=(tt == NKT - 1)),
                 waits=[(kld.sem, kld.v), ev], inc=mm)
            p.on("tensor",
                 lambda e, tt=tt: e.matmul(
                     z_ps[:, :], ka_t[tt % 2][:, 0:64], C["ones_col_bf"],
                     start=(tt == 0), stop=(tt == NKT - 1)),
                 waits=[(kld.sem, kld.v), ev], inc=mm)
            marks.append(mm.v)
        p.on("vector",
             lambda e: e.tensor_copy(kv_bf[0:64, :], kvz_ps[0:64, 0:64]),
             waits=(mm,), inc=ev)
        p.on("vector",
             lambda e: e.tensor_copy(kv_bf[64:128, :], kvz_ps[0:64, 0:64]),
             waits=(mm,), inc=ev)
        p.on("vector",
             lambda e: e.tensor_copy(z_bf[0:64, :], z_ps[:, :]),
             waits=(mm,), inc=ev)
        p.on("vector",
             lambda e: e.tensor_copy(z_bf[64:128, :], z_ps[:, :]),
             waits=(mm,), inc=ev)
        for qbl in range(NTB):
            sl = slice(qbl * TB, (qbl + 1) * TB)
            p.on("tensor",
                 lambda e, fo=fo, po=po, sl=sl: e.matmul(
                     n_ps[:, :], kv_bf[po:po + 64, :], q_sb[fo][po:po + 64, sl],
                     start=True, stop=True),
                 waits=(ev,), inc=mm)
            p.on("tensor",
                 lambda e, fo=fo, po=po, sl=sl: e.matmul(
                     d_ps[0:1, :], z_bf[po:po + 64, :], q_sb[fo][po:po + 64, sl],
                     start=True, stop=True),
                 waits=(ev,), inc=mm)
            p.on("vector",
                 lambda e: e.tensor_scalar_add(drow[:, :], d_ps[0:1, :], 1e-6),
                 waits=(mm,), inc=ev)
            p.on("vector", lambda e: e.reciprocal(drow[:, :], drow[:, :]),
                 waits=(ev,), inc=ev)
            p.on("tensor",
                 lambda e: e.matmul(b_ps[:, :], C["ones_row"][:, 0:64],
                                    drow[:, :], start=True, stop=True),
                 waits=(ev,), inc=mm)
            p.on("vector", lambda e: e.tensor_copy(bc_sb[:, :], b_ps[:, :]),
                 waits=(mm,), inc=ev)
            p.on("vector",
                 lambda e, fo=fo, po=po, sl=sl: e.tensor_tensor(
                     ao_sb[fo][po:po + 64, sl], n_ps[:, :], bc_sb[:, :],
                     ALU.mult),
                 waits=(ev,), inc=ev)
    for f in range(NF):
        p.dma("sync", lambda f=f: ao_d[f * 128:(f + 1) * 128, :],
              lambda f=f: ao_sb[f][:, :], st, waits=(ev,))
    p.on("gpsimd", lambda e: e.nop(), waits=(st,))
    return p.emit()



def _cols(b, n):
    return np.ascontiguousarray(np.asarray(b, np.float32).reshape(n, 128).T)


def _chunks_T(x):
    return [np.ascontiguousarray(x[c * CHUNK:(c + 1) * CHUNK].T)
            for c in range(NCORE)]


def _gather_T(res, key):
    return np.concatenate(
        [np.asarray(r[key], np.float32).T for r in res], 0)


def _run(nc, in_maps):
    return run_bass_kernel_spmd(nc, in_maps, core_ids=list(range(NCORE)))


def kernel(**inputs):
    g = lambda n: np.asarray(inputs[n], np.float32)
    consts = host_consts()
    x = g("L0_attn_a_x").reshape(TOK, D)

    ncA = build_A()
    xc = _chunks_T(x)
    resA = _run(ncA, [
        {"xT": xc[c], "qkv_w": g("L0_attn_a_qkv_w"),
         "qkv_b_cols": _cols(g("L0_attn_a_qkv_b"), 24), **consts}
        for c in range(NCORE)]).results
    qn = _gather_T(resA, "qnT")
    kn = _gather_T(resA, "knT")
    vn = _gather_T(resA, "vnT")

    ncB = build_B()
    qnc = _chunks_T(qn)
    resB = _run(ncB, [
        {"xT": xc[c], "qnT": qnc[c],
         "kn_tok": np.ascontiguousarray(kn[(c // 4) * SEQ:(c // 4 + 1) * SEQ]),
         "vn_tok": np.ascontiguousarray(vn[(c // 4) * SEQ:(c // 4 + 1) * SEQ]),
         "proj_w": g("L0_attn_a_proj_w"),
         "proj_b_cols": _cols(g("L0_attn_a_proj_b"), 8), **consts}
        for c in range(NCORE)]).results
    x1 = _gather_T(resB, "x1T")
    s0 = _gather_T(resB, "s0T")
    sums = np.stack([np.asarray(r["sums"], np.float32) for r in resB], 0)

    ncD = build_D()
    ffn_w = {
        "cn_w1": g("L0_ffn_cn_w1"), "cn_w2": g("L0_ffn_cn_w2"),
        "cn_w3": g("L0_ffn_cn_w3"), "cn_w4": g("L0_ffn_cn_w4"),
        "cn_b1_cols": _cols(g("L0_ffn_cn_b1"), 8),
        "cn_b2_cols": _cols(g("L0_ffn_cn_b2"), 4),
        "cn_b3_cols": _cols(g("L0_ffn_cn_b3"), 2),
        "cn_b4": g("L0_ffn_cn_b4").reshape(1, 1),
        "gc_w1": g("L0_ffn_gc_w1"), "gc_w2": g("L0_ffn_gc_w2"),
        "gc_w3": g("L0_ffn_gc_w3"),
        "gc_b1_cols": _cols(g("L0_ffn_gc_b1"), 4),
        "gc_b2_cols": _cols(g("L0_ffn_gc_b2"), 2),
        "gc_b3": g("L0_ffn_gc_b3").reshape(1, 1),
        "ar": g("L0_ffn_adaptive_range").reshape(1, 1),
    }

    def run_ffn(s_arr, x_arr, sums_arr):
        sc = _chunks_T(s_arr)
        xcc = _chunks_T(x_arr)
        res = _run(ncD, [
            {"s0T": sc[c], "x1T": xcc[c],
             "sums4": sums_arr[(c // 4) * 4:(c // 4 + 1) * 4],
             **ffn_w, **consts} for c in range(NCORE)]).results
        return _gather_T(res, "x2T")

    x2 = run_ffn(s0, x1, sums)

    ncE = build_E()
    bkqv = g("L0_attn_b_bkqv")
    bcols = np.zeros((16, 128, 17), np.float32)
    for h in range(16):
        bcols[h, :, 0:16] = bkqv[h, 0:2048].reshape(16, 128).T
        bcols[h, 0:64, 16] = bkqv[h, 2048:2112]
    x2c = _chunks_T(x2)
    resE = _run(ncE, [
        {"xT": x2c[c], "Wkqv": g("L0_attn_b_Wkqv"), "bkqv_cols": bcols,
         **consts} for c in range(NCORE)]).results
    KO = np.stack([np.asarray(r["K_o"], np.float32) for r in resE], 0)
    QO = np.stack([np.asarray(r["Q_o"], np.float32) for r in resE], 0)
    VO = np.stack([np.asarray(r["V_o"], np.float32) for r in resE], 0)

    mask = np.zeros((128, 896), np.float32)
    kkk, jjj = np.meshgrid(np.arange(128), np.arange(896), indexing="ij")
    mask[kkk <= jjj - 384] = 1.0
    ncF = build_F()
    in_maps = []
    for c in range(NCORE):
        b = c // 4
        hs = [c % 4 + 4 * i for i in range(4)]
        Q4 = np.stack([np.concatenate([QO[b * 4 + cc, h] for cc in range(4)],
                                      1) for h in hs], 0)
        K4 = np.stack([np.concatenate([KO[b * 4 + cc, h] for cc in range(4)],
                                      1) for h in hs], 0)
        V4 = np.stack([np.concatenate(
            [np.concatenate([VO[b * 4 + cc, h].T for cc in range(4)], 0),
             np.ones((SEQ, 1), np.float32)], 1) for h in hs], 0)
        in_maps.append({"Q4": np.ascontiguousarray(Q4),
                        "K4": np.ascontiguousarray(K4),
                        "V4": np.ascontiguousarray(V4),
                        "mask": mask, **consts})
    resF = _run(ncF, in_maps).results
    saF = np.stack([np.asarray(r["sa4"], np.float32) for r in resF], 0)
    sa_full = np.zeros((B, D, SEQ), np.float32)
    for c in range(NCORE):
        b = c // 4
        for pi in range(4):
            h = c % 4 + 4 * pi
            sa_full[b, h * 64:(h + 1) * 64, :] = saF[c, pi]

    ncP = build_P()

    def run_P(x_arr, a_chunks, w, b_):
        xcc = _chunks_T(x_arr)
        res = _run(ncP, [
            {"xT": xcc[c], "aT": a_chunks[c], "w": w,
             "b_cols": _cols(b_, 8), **consts}
            for c in range(NCORE)]).results
        return (_gather_T(res, "xoT"), _gather_T(res, "sT"),
                np.stack([np.asarray(r["sums"], np.float32) for r in res], 0))

    sa_chunks = [np.ascontiguousarray(
        sa_full[c // 4, :, (c % 4) * CHUNK:(c % 4 + 1) * CHUNK])
        for c in range(NCORE)]
    x3, s3, sums3 = run_P(x2, sa_chunks, g("L0_attn_b_Wp"),
                          g("L0_attn_b_bp"))

    x4 = run_ffn(s3, x3, sums3)

    ncH = build_H()
    x4c = _chunks_T(x4)
    resH = _run(ncH, [
        {"xT": x4c[c], "Wq": g("L1_attn_a_Wq"), "Wk": g("L1_attn_a_Wk"),
         "Wv": g("L1_attn_a_Wv"), "bq_cols": _cols(g("L1_attn_a_bq"), 8),
         "bk_cols": _cols(g("L1_attn_a_bk"), 8),
         "bv_cols": _cols(g("L1_attn_a_bv"), 8), **consts}
        for c in range(NCORE)]).results
    q1 = _gather_T(resH, "qT")
    k1 = _gather_T(resH, "kT")
    v1 = _gather_T(resH, "vT")

    ncI = build_I()
    q1c = _chunks_T(q1)
    in_maps = []
    for c in range(NCORE):
        b = c // 4
        kT_full = np.ascontiguousarray(k1[b * SEQ:(b + 1) * SEQ].T)
        vb = v1[b * SEQ:(b + 1) * SEQ]
        v_aug = np.stack([np.concatenate(
            [vb[:, h * 64:(h + 1) * 64], np.ones((SEQ, 1), np.float32)], 1)
            for h in range(16)], 0)
        in_maps.append({"qT": q1c[c], "kT_full": kT_full,
                        "v_aug": np.ascontiguousarray(v_aug), **consts})
    resI = _run(ncI, in_maps).results
    ao1 = _gather_T(resI, "aoT")

    x5, s5, _ = run_P(x4, _chunks_T(ao1), g("L1_attn_a_Wo"),
                      g("L1_attn_a_bo"))

    ncR = build_R()
    G1 = [g("L1_ffn_G1_R"), g("L1_ffn_G1_I")]
    G2 = [g("L1_ffn_G2_R"), g("L1_ffn_G2_I")]
    g2lhs = np.zeros((128, 112), np.float32)
    g1rhs = np.zeros((256, 32), np.float32)
    for ri in range(2):
        for c in range(4):
            for gg in range(16):
                g2lhs[(ri * 4 + c) * 16 + gg, 48:64] = G2[ri][c, :, 0, gg]
            for b_ in range(32):
                g1rhs[(ri * 4 + c) * 32 + b_, :] = G1[ri][0, b_, c, :]

    def run_R(s_arr, x_arr):
        sc = _chunks_T(s_arr)
        xcc = _chunks_T(x_arr)
        res = _run(ncR, [
            {"sT": sc[c], "xT": xcc[c], "g2lhs": g2lhs, "g1rhs": g1rhs,
             "br_cols": _cols(g("L1_ffn_bias_real"), 4),
             "bi_cols": _cols(g("L1_ffn_bias_imag"), 4), **consts}
            for c in range(NCORE)]).results
        return _gather_T(res, "xoT"), _gather_T(res, "sT_o")

    x6, s6 = run_R(s5, x5)

    ncK1 = build_K1()
    s6c = _chunks_T(s6)
    resK = _run(ncK1, [
        {"sT": s6c[c], "w_qkv": g("L1_attn_b_w_qkv"), **consts}
        for c in range(NCORE)]).results
    phq = _gather_T(resK, "phqT")
    phk = _gather_T(resK, "phkT")
    vv = _gather_T(resK, "vT")

    ncL = build_L1b()
    phqc = _chunks_T(phq)
    in_maps = []
    for c in range(NCORE):
        b = c // 4
        kb = phk[b * SEQ:(b + 1) * SEQ]
        vb = vv[b * SEQ:(b + 1) * SEQ]
        phk_aug = np.stack([np.concatenate(
            [kb[:, h * 64:(h + 1) * 64], np.ones((SEQ, 1), np.float32)], 1)
            for h in range(16)], 0)
        vphk = np.stack([np.concatenate(
            [vb[:, h * 64:(h + 1) * 64], kb[:, h * 64:(h + 1) * 64]], 1)
            for h in range(16)], 0)
        in_maps.append({"phqT": phqc[c],
                        "phk_aug": np.ascontiguousarray(phk_aug),
                        "vphk": np.ascontiguousarray(vphk), **consts})
    resL = _run(ncL, in_maps).results
    ao2 = _gather_T(resL, "aoT")

    x7, s7, _ = run_P(x6, _chunks_T(ao2), g("L1_attn_b_w_out"),
                      g("L1_attn_b_b_out"))

    x8, _ = run_R(s7, x7)
    return x8.reshape(B, SEQ, D)
